# version 01
# baseline (speedup 1.0000x reference)
import dataclasses
from contextlib import ExitStack

import numpy as np

import concourse.bass as bass
import concourse.mybir as mybir
from concourse.bass_utils import run_bass_kernel_spmd

F32 = mybir.dt.float32
BF16 = mybir.dt.bfloat16
I32 = mybir.dt.int32
AF = mybir.ActivationFunctionType
ALU = mybir.AluOpType
AX = mybir.AxisListType

NCORE = 8
DM = 2048
NCH = 16
TPC = 1024
EPS_LN = 1e-5
EPS_RMS = float(np.finfo(np.float32).eps)
NEG = -1.0e30
SCALE0 = 0.125
SCALE1 = float(1.0 / np.sqrt(128.0))

ENGS = ("tensor", "vector", "scalar", "gpsimd", "sync")
DEBUG0 = False
TRUNC0 = False
TRUNC1 = False
TRUNC2 = False
XNROVR = False
NFLUSH = 0


class _Trunc(Exception):
    pass


def bc0(ap, axis, n):
    newap = [list(x) for x in ap.ap]
    assert newap[axis][1] == 1, (newap, axis)
    newap[axis] = [0, n]
    return dataclasses.replace(ap, ap=newap)


class Plan:
    def __init__(self):
        self.ops = {e: [] for e in ENGS}
        self.cnt = {}
        self.guards = {}

    def val(self, s):
        return self.cnt.get(s, 0)

    muted = False

    def emit(self, eng, meth, *args, wait=(), inc=(), **kw):
        if self.muted:
            return
        for s, v in wait:
            assert v <= self.val(s), (eng, meth, s, v, self.val(s))
        for s, n in inc:
            self.cnt[s] = self.val(s) + n
        self.ops[eng].append((meth, args, kw, tuple(wait), tuple(inc)))

    def W(self, name):
        return tuple(self.guards.pop(name, ()))

    def R(self, name, sem):
        g = self.guards.setdefault(name, [])
        g.append((sem, self.val(sem)))

    def after(self, *sems):
        return tuple((s, self.val(s)) for s in sems)


def playback(nc, ctx, plan):
    sems = {name: ctx.enter_context(nc.semaphore(name)) for name in plan.cnt}
    block = ctx.enter_context(nc.Block())
    for eng_name in ("sync", "gpsimd", "scalar", "vector", "tensor"):
        ops = plan.ops[eng_name]
        if not ops:
            continue
        dec = getattr(block, eng_name)

        def make(ops_):
            def body(eng):
                for meth, args, kw, waits, incs in ops_:
                    for s, v in waits:
                        if v > 0:
                            eng.wait_ge(sems[s], v)
                    ins = getattr(eng, meth)(*args, **kw)
                    for s, n in incs:
                        ins.then_inc(sems[s], n)
            return body
        dec(make(ops))


def seed(P, names):
    f = P.after("PE", "VE", "AC", "LD", "LC", "TR", "ST")
    for n in names:
        P.guards.setdefault(n, []).extend(f)


def tiles(total, size):
    out = []
    o = 0
    while o < total:
        out.append((o, min(size, total - o)))
        o += size
    return out



class Env:
    pass


def alloc(env, ctx, name, shape, dt):
    env.seq = getattr(env, "seq", 0) + 1
    return ctx.enter_context(env.nc.sbuf_tensor(f"{name}_{env.seq}", shape, dt))


def psum(env, ctx, name, dt=F32):
    env.seq = getattr(env, "seq", 0) + 1
    shape = [128, 512] if dt == F32 else [128, 1024]
    return ctx.enter_context(env.nc.psum_tensor(f"{name}_{env.seq}", shape, dt))


def emit_ln_setup(env, sub):
    env.s_xb = alloc(env, sub, "ln_xb", [128, 2, 512], BF16)
    env.s_xq = alloc(env, sub, "ln_xq", [128, 2, 512], BF16)
    env.st_m = alloc(env, sub, "ln_m", [1, 512], F32)
    env.st_q = alloc(env, sub, "ln_q", [1, 512], F32)
    env.st_t = alloc(env, sub, "ln_t", [1, 512], F32)
    env.st_t2 = alloc(env, sub, "ln_t2", [1, 512], F32)
    env.st_A = alloc(env, sub, "ln_A", [1, 512], F32)
    env.st_B = alloc(env, sub, "ln_B", [1, 512], F32)
    env.s_t1 = alloc(env, sub, "ln_t1", [128, 512], F32)
    env.p_s = psum(env, sub, "ln_ps")
    env.p_q = psum(env, sub, "ln_pq")
    env.p_A = psum(env, sub, "ln_pA")
    env.p_B = psum(env, sub, "ln_pB")


def emit_ln(env, tag, x3, xn3, T, dep):
    P = env.P
    for t0, tw in tiles(T, 512):
        sl = slice(t0, t0 + tw)
        for c in range(NCH):
            s = c & 1
            P.emit("vector", "tensor_copy", env.s_xb[:, s, 0:tw], x3[:, c, sl],
                   wait=P.W(f"{tag}xb{s}") + (dep if (t0 == 0 and c == 0) else ()),
                   inc=[("VE", 1)])
            P.emit("scalar", "activation", env.s_xq[:, s, 0:tw], env.s_xb[:, s, 0:tw],
                   AF.Square, wait=P.after("VE") + P.W(f"{tag}xq{s}"), inc=[("AC", 1)])
            w = P.after("AC")
            if c == 0:
                w = w + P.W(f"{tag}ps") + P.W(f"{tag}pq")
            P.emit("tensor", "matmul", env.p_s[0:1, 0:tw], env.ones_b[:, :],
                   env.s_xb[:, s, 0:tw], start=(c == 0), stop=(c == 15),
                   skip_group_check=True, wait=w)
            P.emit("tensor", "matmul", env.p_q[0:1, 0:tw], env.ones_b[:, :],
                   env.s_xq[:, s, 0:tw], start=(c == 0), stop=(c == 15),
                   skip_group_check=True, inc=[("PE", 1)])
            P.guards.setdefault(f"{tag}xb{s}", []).append(("PE", P.val("PE")))
            P.guards.setdefault(f"{tag}xq{s}", []).append(("PE", P.val("PE")))
        P.emit("vector", "tensor_scalar", env.st_m[:, 0:tw], env.p_s[0:1, 0:tw],
               1.0 / DM, None, op0=ALU.mult, wait=P.after("PE"))
        P.emit("vector", "tensor_scalar", env.st_q[:, 0:tw], env.p_q[0:1, 0:tw],
               1.0 / DM, EPS_LN, op0=ALU.mult, op1=ALU.add)
        P.emit("vector", "tensor_tensor", env.st_t[:, 0:tw], env.st_m[:, 0:tw],
               env.st_m[:, 0:tw], op=ALU.mult)
        P.emit("vector", "tensor_tensor", env.st_t2[:, 0:tw], env.st_q[:, 0:tw],
               env.st_t[:, 0:tw], op=ALU.subtract, inc=[("VE", 1)])
        P.R(f"{tag}ps", "VE")
        P.R(f"{tag}pq", "VE")
        P.emit("scalar", "activation", env.st_t[:, 0:tw], env.st_t2[:, 0:tw],
               AF.Ln, wait=P.after("VE"))
        P.emit("scalar", "activation", env.st_A[:, 0:tw], env.st_t[:, 0:tw],
               AF.Exp, scale=-0.5, inc=[("AC", 1)],
               wait=P.W(f"{tag}stA"))
        P.emit("vector", "scalar_tensor_tensor", env.st_B[:, 0:tw], env.st_m[:, 0:tw],
               -1.0, env.st_A[:, 0:tw], op0=ALU.mult, op1=ALU.mult,
               wait=P.after("AC") + P.W(f"{tag}stB"), inc=[("VE", 1)])
        P.emit("tensor", "matmul", env.p_A[:, 0:tw], env.ones_f[:, :],
               env.st_A[:, 0:tw], start=True, stop=True,
               wait=P.after("VE") + P.W(f"{tag}pA"))
        P.emit("tensor", "matmul", env.p_B[:, 0:tw], env.ones_f[:, :],
               env.st_B[:, 0:tw], start=True, stop=True,
               wait=P.W(f"{tag}pB"), inc=[("PE", 1)])
        P.R(f"{tag}stA", "PE")
        P.R(f"{tag}stB", "PE")
        for c in range(NCH):
            P.emit("vector", "tensor_tensor", env.s_t1[:, 0:tw], x3[:, c, sl],
                   env.p_A[:, 0:tw], op=ALU.mult,
                   wait=P.after("PE") if c == 0 else ())
            P.emit("vector", "tensor_tensor", xn3[:, c, sl], env.s_t1[:, 0:tw],
                   env.p_B[:, 0:tw], op=ALU.add,
                   inc=[("VE", 1)] if c == 15 else ())
        P.R(f"{tag}pA", "VE")
        P.R(f"{tag}pB", "VE")



def plan_block0(nc, ctx):
    xT = nc.dram_tensor("xT", [DM, TPC], F32, kind="ExternalInput").ap()
    wqkv = nc.dram_tensor("w_qkv", [DM, 3 * DM], F32, kind="ExternalInput").ap()
    wout = nc.dram_tensor("w_out", [DM, DM], F32, kind="ExternalInput").ap()
    wnpp = nc.dram_tensor("w_norm_pp", [128, 16], F32, kind="ExternalInput").ap()
    pmk = nc.dram_tensor("pmkT", [128, 256], F32, kind="ExternalInput").ap()
    pmv = nc.dram_tensor("pmvT", [16, DM], F32, kind="ExternalInput").ap()
    wg = nc.dram_tensor("w_gate", [8192, DM], F32, kind="ExternalInput").ap()
    wu = nc.dram_tensor("w_up", [8192, DM], F32, kind="ExternalInput").ap()
    wd = nc.dram_tensor("w_down", [DM, 8192], F32, kind="ExternalInput").ap()
    mask = nc.dram_tensor("mask0", [128, 144], F32, kind="ExternalInput").ap()
    iden = nc.dram_tensor("ident", [128, 128], F32, kind="ExternalInput").ap()
    x1T = nc.dram_tensor("x1T", [DM, TPC], F32, kind="ExternalOutput").ap()
    if XNROVR:
        xnr_host = nc.dram_tensor("xnr_host", [DM, 512], F32, kind="ExternalInput").ap()
    if DEBUG0:
        d_wq0 = nc.dram_tensor("d_wq0", [DM, 128], BF16, kind="ExternalOutput").ap()
        d_xn = nc.dram_tensor("d_xn", [DM, TPC], BF16, kind="ExternalOutput").ap()
        d_xa = nc.dram_tensor("d_xa", [DM, TPC], F32, kind="ExternalOutput").ap()
        d_xnr = nc.dram_tensor("d_xnr", [DM, 512], BF16, kind="ExternalOutput").ap()
        d_qT = nc.dram_tensor("d_qT", [DM, 512], BF16, kind="ExternalOutput").ap()
        d_kT = nc.dram_tensor("d_kT", [DM, 512], BF16, kind="ExternalOutput").ap()
        d_vtok = nc.dram_tensor("d_vtok", [DM, 512], BF16, kind="ExternalOutput").ap()
        d_vT = nc.dram_tensor("d_vT", [DM, 512], BF16, kind="ExternalOutput").ap()
        d_oT = nc.dram_tensor("d_oT", [DM, 512], BF16, kind="ExternalOutput").ap()

    env = Env()
    env.nc = nc
    env.ctx = ctx
    env.P = P = Plan()

    env.s_xres = alloc(env, env.ctx, "xres", [128, NCH, TPC], F32)
    env.s_xnT = alloc(env, env.ctx, "xnT", [128, NCH, TPC], BF16)
    env.ones_b = alloc(env, env.ctx, "ones_b", [128, 1], BF16)
    env.ones_f = alloc(env, env.ctx, "ones_f", [1, 128], F32)
    s_wn = alloc(env, env.ctx, "wn", [128, 16], F32)
    s_mask = alloc(env, env.ctx, "mask", [128, 144], F32)
    s_pmk = alloc(env, env.ctx, "pmk", [128, 256], BF16)
    s_pmv = alloc(env, env.ctx, "pmv", [16, DM], BF16)
    s_id = alloc(env, env.ctx, "id", [128, 128], BF16)

    xr = xT.rearrange("(c p) t -> p c t", p=128)
    P.emit("scalar", "dma_start", out=env.s_xres[:, :, :], in_=xr, inc=[("LD", 16)])
    P.emit("scalar", "dma_start", out=s_wn[:, :], in_=wnpp, inc=[("LD", 16)])
    P.emit("scalar", "dma_start", out=s_mask[:, :], in_=mask, inc=[("LD", 16)])
    if not TRUNC0:
        P.emit("gpsimd", "dma_start", out=s_pmk[:, :], in_=pmk, inc=[("LC", 16)])
        P.emit("gpsimd", "dma_start", out=s_pmv[:, :], in_=pmv, inc=[("LC", 16)])
        P.emit("gpsimd", "dma_start", out=s_id[:, :], in_=iden, inc=[("LC", 16)])
    P.emit("vector", "memset", env.ones_b[:, :], 1.0, inc=[("VE", 1)])
    P.emit("vector", "memset", env.ones_f[:, :], 1.0, inc=[("VE", 1)])

    if TRUNC1 or TRUNC2:
        P.muted = True

    with ExitStack() as sub:
        emit_ln_setup(env, sub)
        emit_ln(env, "l0", env.s_xres, env.s_xnT, TPC,
                dep=(("LD", 16),) + P.after("VE"))

    if DEBUG0:
        P.emit("sync", "dma_start", out=d_xn.rearrange("(c p) t -> p c t", p=128),
               in_=env.s_xnT[:, :, :], wait=P.after("VE"), inc=[("ST", 16)])

    with ExitStack() as attn:
        s_xnr = alloc(env, attn, "xnr", [128, NCH, 512], BF16)
        s_qT = alloc(env, attn, "qT0", [128, NCH, 512], BF16)
        if DEBUG0:
            P.emit("vector", "memset", s_qT[:, :, :], 0.0, inc=[("VE", 1)])
        s_kT = alloc(env, attn, "kT0", [128, NCH, 512], BF16)
        s_vtok = alloc(env, attn, "vtok0", [128, NCH, 4, 128], BF16)
        seed(P, [f"vtok{c}" for c in range(NCH)])
        p_A = psum(env, attn, "a_pA")
        p_B = psum(env, attn, "a_pB")
        p_C = psum(env, attn, "a_pC")
        p_D = psum(env, attn, "a_pD")
        p_E0 = psum(env, attn, "a_pE0", BF16)
        p_E1 = psum(env, attn, "a_pE1", BF16)
        p_F = psum(env, attn, "a_pF")
        st_r = alloc(env, attn, "st_r", [1, 512], F32)

        for half in range(2):
            hsl = slice(512 * half, 512 * half + 512)
            if TRUNC2 and half == 0:
                P.muted = False
            with ExitStack() as sub:
                s_sq = alloc(env, sub, "rms_sq", [128, 2, 512], BF16)
                seed(P, ["rsq0", "rsq1"])
                for c in range(NCH):
                    s = c & 1
                    P.emit("scalar", "activation", s_sq[:, s, :], env.s_xnT[:, c, hsl],
                           AF.Square, wait=(P.after("VE") if c == 0 else ()) + P.W(f"rsq{s}"),
                           inc=[("AC", 1)])
                    P.emit("tensor", "matmul", p_F[0:1, 0:512], env.ones_b[:, :],
                           s_sq[:, s, :], start=(c == 0), stop=(c == 15),
                           wait=P.after("AC") + (P.W("a_pF") if c == 0 else ()),
                           inc=[("PE", 1)] if c == 15 else ())
                    P.guards.setdefault(f"rsq{s}", []).append(("PE", P.val("PE")))
                P.emit("vector", "tensor_scalar", st_r[:, 0:512], p_F[0:1, 0:512],
                       1.0 / DM, EPS_RMS, op0=ALU.mult, op1=ALU.add,
                       wait=P.after("PE") + P.W("st_r"), inc=[("VE", 1)])
                P.R("a_pF", "VE")
                P.emit("scalar", "activation", st_r[:, 0:512], st_r[:, 0:512],
                       AF.Ln, wait=P.after("VE"))
                P.emit("scalar", "activation", st_r[:, 0:512], st_r[:, 0:512],
                       AF.Exp, scale=-0.5, inc=[("AC", 1)])
                P.emit("tensor", "matmul", p_B[:, 0:512], env.ones_f[:, :],
                       st_r[:, 0:512], start=True, stop=True,
                       wait=P.after("AC") + P.W("a_pB"), inc=[("PE", 1)])
                P.R("st_r", "PE")
                for c in range(NCH):
                    P.emit("vector", "scalar_tensor_tensor", s_xnr[:, c, :],
                           env.s_xnT[:, c, hsl], s_wn[:, c:c + 1], p_B[:, 0:512],
                           op0=ALU.mult, op1=ALU.mult,
                           wait=(P.after("PE") + (("LD", 32),) + P.W("xnr")) if c == 0 else (),
                           inc=[("VE", 1)] if c == 15 else ())
                P.R("a_pB", "VE")
                for _fl in range(NFLUSH):
                    P.emit("vector", "tensor_copy", s_xnr[:, 15, 0:8],
                           s_xnr[:, 15, 0:8], inc=[("VE", 1)])
                if XNROVR and half == 0:
                    P.emit("gpsimd", "dma_start", out=s_xnr[:, :, :],
                           in_=xnr_host.rearrange("(c p) t -> p c t", p=128),
                           wait=P.after("VE"), inc=[("LC", 16)])

            if TRUNC1:
                P.muted = False
            with ExitStack() as sub:
                s_wq = alloc(env, sub, "wqbuf", [128, 2, NCH, 128], BF16)
                s_vT = alloc(env, sub, "vT0", [128, NCH, 512], BF16)
                seed(P, ["wq0", "wq1"])
                wqr = wqkv.rearrange("(c p) o -> p c o", p=128)
                pq = [p_A, p_C]
                for ob in range(48):
                    s = ob & 1
                    P.emit("gpsimd", "dma_start", out=s_wq[:, s, :, :],
                           in_=wqr[:, :, 128 * ob:128 * (ob + 1)],
                           wait=P.W(f"wq{s}"), inc=[("LC", 16)])
                    lc = P.val("LC")
                    for c in range(NCH):
                        w = ()
                        if c == 0:
                            w = (("LC", lc),) + P.W(f"a_p{'AC'[s]}") + \
                                (P.after("VE", "AC", "PE", "LD", "TR") if ob == 0 else ())
                        P.emit("tensor", "matmul", pq[s][:, 0:512],
                               s_wq[:, s, c, :], s_xnr[:, c, :],
                               start=(c == 0), stop=(c == 15), wait=w,
                               inc=[("PE", 1)] if c == 15 else ())
                    P.guards.setdefault(f"wq{s}", []).append(("PE", P.val("PE")))
                    if DEBUG0 and ob == 0 and half == 0:
                        P.emit("sync", "dma_start",
                               out=d_wq0.rearrange("(c p) o -> p c o", p=128),
                               in_=s_wq[:, 0, :, :], wait=P.after("PE"),
                               inc=[("ST", 16)])
                        P.emit("gpsimd", "nop", wait=P.after("ST"))
                    if ob < 16:
                        P.emit("vector", "tensor_scalar", s_qT[:, ob, :],
                               pq[s][:, 0:512], SCALE0, None, op0=ALU.mult,
                               wait=P.after("PE"), inc=[("VE", 1)])
                    elif ob < 32:
                        P.emit("vector", "tensor_copy", s_kT[:, ob - 16, :],
                               pq[s][:, 0:512], wait=P.after("PE"), inc=[("VE", 1)])
                    else:
                        P.emit("vector", "tensor_copy", s_vT[:, ob - 32, :],
                               pq[s][:, 0:512], wait=P.after("PE"), inc=[("VE", 1)])
                    P.R(f"a_p{'AC'[s]}", "VE")
                P.R("xnr", "PE")
                if DEBUG0 and half == 0:
                    P.emit("sync", "dma_start", out=d_vT.rearrange("(c p) t -> p c t", p=128),
                           in_=s_vT[:, :, :], wait=P.after("VE"), inc=[("ST", 16)])
                    P.emit("sync", "dma_start", out=d_xnr.rearrange("(c p) t -> p c t", p=128),
                           in_=s_xnr[:, :, :], wait=P.after("VE"), inc=[("ST", 16)])
                    P.emit("sync", "dma_start", out=d_qT.rearrange("(c p) t -> p c t", p=128),
                           in_=s_qT[:, :, :], wait=P.after("VE"), inc=[("ST", 16)])
                    P.emit("sync", "dma_start", out=d_kT.rearrange("(c p) t -> p c t", p=128),
                           in_=s_kT[:, :, :], wait=P.after("VE"), inc=[("ST", 16)])
                if TRUNC0:
                    P.muted = True
                ve = P.val("VE")
                for c in range(NCH):
                    P.emit("sync", "dma_start_transpose", s_vtok[:, c, :, :],
                           s_vT[:, c, :], wait=(("VE", ve),) + P.W(f"vtok{c}"),
                           inc=[("TR", 16)])

            if DEBUG0 and half == 0:
                P.emit("sync", "dma_start",
                       out=d_vtok.rearrange("(c p) t -> p c t", p=128),
                       in_=s_vtok.ap().rearrange("p c m r -> p c (m r)"),
                       wait=P.after("TR"), inc=[("ST", 16)])
            with ExitStack() as sub:
                s_sm = alloc(env, sub, "sm", [128, 2, 144], F32)
                seed(P, ["sm0", "sm1", "au0", "au1", "an0", "an1", "at0", "at1"])
                s_nm = alloc(env, sub, "nm", [128, 2], F32)
                s_au = alloc(env, sub, "au", [128, 2, 144], BF16)
                s_rs = alloc(env, sub, "rs", [128, 2], F32)
                s_rn = alloc(env, sub, "rn", [128, 2], F32)
                s_an = alloc(env, sub, "an", [128, 2, 144], BF16)
                s_at = alloc(env, sub, "at", [128, 2, 256], BF16)
                s_oT = alloc(env, sub, "oT0", [128, NCH, 512], BF16)
                p_sc = [p_A, p_C]
                p_at = [p_E0, p_E1]
                p_av = [p_B, p_D]
                trv = P.val("TR")
                for it, (h, tl) in enumerate(
                        [(h, t) for h in range(32) for t in range(4)]):
                    s = it & 1
                    a = h & 1
                    ch = h >> 1
                    qsl = (slice(64 * a, 64 * a + 64), ch,
                           slice(128 * tl, 128 * tl + 128))
                    w = P.W(f"a_p{'AC'[s]}")
                    if it == 0:
                        w = w + P.after("VE") + (("LC", 48),) + (("LD", 48),)
                    P.emit("tensor", "matmul", p_sc[s][:, 0:128],
                           s_qT[qsl], s_kT[qsl], start=True, stop=False, wait=w)
                    P.emit("tensor", "matmul", p_sc[s][:, 128:144],
                           s_qT[qsl], s_pmk[64 * a:64 * a + 64, 16 * ch:16 * ch + 16],
                           start=False, stop=True, inc=[("PE", 1)])
                    P.emit("vector", "tensor_tensor", s_sm[:, s, :],
                           p_sc[s][:, 0:144], s_mask[:, :], op=ALU.add,
                           wait=P.after("PE") + P.W(f"sm{s}"))
                    P.emit("vector", "tensor_reduce", s_nm[:, s:s + 1],
                           s_sm[:, s, :], axis=AX.X, op=ALU.max, negate=True,
                           inc=[("VE", 1)])
                    P.guards.setdefault(f"a_p{'AC'[s]}", []).append(("VE", P.val("VE")))
                    P.emit("scalar", "activation", s_au[:, s, :], s_sm[:, s, :],
                           AF.Exp, bias=s_nm[:, s:s + 1],
                           accum_out=s_rs[:, s:s + 1],
                           wait=P.after("VE") + P.W(f"au{s}"), inc=[("AC", 1)])
                    P.guards.setdefault(f"sm{s}", []).append(("AC", P.val("AC")))
                    P.emit("vector", "reciprocal", s_rn[:, s:s + 1], s_rs[:, s:s + 1],
                           wait=P.after("AC"))
                    P.emit("vector", "tensor_scalar", s_an[:, s, :], s_au[:, s, :],
                           s_rn[:, s:s + 1], None, op0=ALU.mult,
                           wait=P.W(f"an{s}"), inc=[("VE", 1)])
                    P.guards.setdefault(f"au{s}", []).append(("VE", P.val("VE")))
                    w = P.after("VE") + P.W(f"a_pE{s}")
                    P.emit("tensor", "transpose", p_at[s][:, 0:128],
                           s_an[:, s, 0:128], s_id[:, :], wait=w)
                    P.emit("tensor", "transpose", p_at[s][0:16, 128:256],
                           s_an[:, s, 128:144], s_id[:, :], inc=[("PE", 1)])
                    P.guards.setdefault(f"an{s}", []).append(("PE", P.val("PE")))
                    P.emit("vector", "tensor_copy", s_at[:, s, :], p_at[s][:, 0:256],
                           wait=P.after("PE") + P.W(f"at{s}"), inc=[("VE", 1)])
                    P.R(f"a_pE{s}", "VE")
                    w = P.after("VE") + P.W(f"a_p{'BD'[s]}")
                    if it == 0:
                        w = w + (("TR", trv),) + (("LC", 32),)
                    tp = (0, 64) if a else None
                    osl = slice(64 * a, 64 * a + 64)
                    P.emit("tensor", "matmul", p_av[s][osl, 0:128],
                           s_vtok[:, ch, tl, 64 * a:64 * a + 64], s_at[:, s, 0:128],
                           start=True, stop=False, tile_position=tp, wait=w)
                    P.emit("tensor", "matmul", p_av[s][osl, 0:128],
                           s_pmv[0:16, 64 * h:64 * h + 64], s_at[0:16, s, 128:256],
                           start=False, stop=True, tile_position=tp, inc=[("PE", 1)])
                    P.guards.setdefault(f"at{s}", []).append(("PE", P.val("PE")))
                    P.guards.setdefault(f"vtok{ch}", []).append(("PE", P.val("PE")))
                    P.emit("vector", "tensor_copy",
                           s_oT[osl, ch, 128 * tl:128 * tl + 128],
                           p_av[s][osl, 0:128],
                           wait=P.after("PE"), inc=[("VE", 1)])
                    P.R(f"a_p{'BD'[s]}", "VE")

                if DEBUG0 and half == 0:
                    P.emit("sync", "dma_start", out=d_oT.rearrange("(c p) t -> p c t", p=128),
                           in_=s_oT[:, :, :], wait=P.after("VE"), inc=[("ST", 16)])
                s_wo = alloc(env, sub, "wobuf", [128, 2, NCH, 128], BF16)
                seed(P, ["wo0", "wo1"])
                wor = wout.rearrange("(c p) o -> p c o", p=128)
                pw = [p_A, p_C]
                ove = P.val("VE")
                for dt in range(NCH):
                    s = dt & 1
                    P.emit("gpsimd", "dma_start", out=s_wo[:, s, :, :],
                           in_=wor[:, :, 128 * dt:128 * (dt + 1)],
                           wait=P.W(f"wo{s}"), inc=[("LC", 16)])
                    lc = P.val("LC")
                    for c in range(NCH):
                        w = ()
                        if c == 0:
                            w = (("LC", lc), ("VE", ove)) + P.W(f"a_p{'AC'[s]}")
                        P.emit("tensor", "matmul", pw[s][:, 0:512],
                               s_wo[:, s, c, :], s_oT[:, c, :],
                               start=(c == 0), stop=(c == 15), wait=w,
                               inc=[("PE", 1)] if c == 15 else ())
                    P.guards.setdefault(f"wo{s}", []).append(("PE", P.val("PE")))
                    P.emit("vector", "tensor_tensor", env.s_xres[:, dt, hsl],
                           env.s_xres[:, dt, hsl], pw[s][:, 0:512], op=ALU.add,
                           wait=P.after("PE"), inc=[("VE", 1)])
                    P.R(f"a_p{'AC'[s]}", "VE")

    if DEBUG0:
        P.emit("sync", "dma_start", out=d_xa.rearrange("(c p) t -> p c t", p=128),
               in_=env.s_xres[:, :, :], wait=P.after("VE"), inc=[("ST", 16)])

    with ExitStack() as ffn:
        s_gatedT = alloc(env, ffn, "gatedT", [128, 32, TPC], BF16)
        p_A = psum(env, ffn, "f_pA")
        p_B = psum(env, ffn, "f_pB")
        p_C = psum(env, ffn, "f_pC")
        p_D = psum(env, ffn, "f_pD")
        for HH in range(2):
            hoff = 4096 * HH
            with ExitStack() as sub:
                s_graw = alloc(env, sub, "graw", [128, 2, DM], BF16)
                s_uraw = alloc(env, sub, "uraw", [128, 2, DM], BF16)
                s_gT = alloc(env, sub, "gT", [128, 2, NCH, 128], BF16)
                s_uT = alloc(env, sub, "uT", [128, 2, NCH, 128], BF16)
                s_sg = alloc(env, sub, "sg", [128, 2, 512], F32)
                seed(P, ["graw0", "graw1", "uraw0", "uraw1", "gT0", "gT1",
                         "uT0", "uT1", "sg0", "sg1"])
                pg = [p_A, p_C]
                pu = [p_B, p_D]
                for hb in range(32):
                    s = hb & 1
                    rows = slice(hoff + 128 * hb, hoff + 128 * (hb + 1))
                    P.emit("gpsimd", "dma_start", out=s_graw[:, s, :],
                           in_=wg[rows, :], wait=P.W(f"graw{s}"), inc=[("LC", 16)])
                    P.emit("gpsimd", "dma_start", out=s_uraw[:, s, :],
                           in_=wu[rows, :], wait=P.W(f"uraw{s}"), inc=[("LC", 16)])
                    lc = P.val("LC")
                    P.emit("sync", "dma_start_transpose", s_gT[:, s, :, :],
                           s_graw[:, s, :], wait=(("LC", lc),) + P.W(f"gT{s}"),
                           inc=[("TR", 16)])
                    P.guards.setdefault(f"graw{s}", []).append(("TR", P.val("TR")))
                    P.emit("sync", "dma_start_transpose", s_uT[:, s, :, :],
                           s_uraw[:, s, :], wait=P.W(f"uT{s}"), inc=[("TR", 16)])
                    P.guards.setdefault(f"uraw{s}", []).append(("TR", P.val("TR")))
                    tr = P.val("TR")
                    for mi, (sw, pp) in enumerate(((s_gT, pg), (s_uT, pu))):
                        for th in range(2):
                            tsl = slice(512 * th, 512 * th + 512)
                            for c in range(NCH):
                                w = ()
                                if c == 0:
                                    w = (("TR", tr),) + P.W(f"f_p{'ACBD'[2 * mi + th]}")
                                    if hb == 0 and mi == 0 and th == 0:
                                        w = w + P.after("VE") + P.W("gatedT")
                                P.emit("tensor", "matmul",
                                       pp[th][:, 0:512], sw[:, s, c, :],
                                       env.s_xnT[:, c, tsl],
                                       start=(c == 0), stop=(c == 15), wait=w,
                                       inc=[("PE", 1)] if c == 15 else ())
                            if mi == 0:
                                P.guards.setdefault(f"gT{s}", []).append(("PE", P.val("PE")))
                            else:
                                P.guards.setdefault(f"uT{s}", []).append(("PE", P.val("PE")))
                    for th in range(2):
                        tsl = slice(512 * th, 512 * th + 512)
                        P.emit("scalar", "activation", s_sg[:, th, :],
                               pg[th][:, 0:512], AF.Silu,
                               wait=P.after("PE") + P.W(f"sg{th}"), inc=[("AC", 1)])
                        P.R(f"f_p{'AC'[th]}", "AC")
                        P.emit("vector", "tensor_tensor",
                               s_gatedT[:, hb, tsl], s_sg[:, th, :],
                               pu[th][:, 0:512], op=ALU.mult,
                               wait=P.after("AC"), inc=[("VE", 1)])
                        P.R(f"f_p{'BD'[th]}", "VE")
                        P.guards.setdefault(f"sg{th}", []).append(("VE", P.val("VE")))
            with ExitStack() as sub:
                s_draw = alloc(env, sub, "draw", [128, 2, 4096], BF16)
                s_dT = alloc(env, sub, "dT", [128, 2, 32, 128], BF16)
                seed(P, ["draw0", "draw1", "dT0", "dT1"])
                pd = [p_A, p_C]
                gve = P.val("VE")
                for dt in range(NCH):
                    s = dt & 1
                    P.emit("gpsimd", "dma_start", out=s_draw[:, s, :],
                           in_=wd[128 * dt:128 * (dt + 1), hoff:hoff + 4096],
                           wait=P.W(f"draw{s}"), inc=[("LC", 16)])
                    lc = P.val("LC")
                    P.emit("sync", "dma_start_transpose", s_dT[:, s, :, :],
                           s_draw[:, s, :], wait=(("LC", lc),) + P.W(f"dT{s}"),
                           inc=[("TR", 16)])
                    P.guards.setdefault(f"draw{s}", []).append(("TR", P.val("TR")))
                    tr = P.val("TR")
                    for th in range(2):
                        tsl = slice(512 * th, 512 * th + 512)
                        for k in range(32):
                            w = ()
                            if k == 0:
                                w = (("TR", tr), ("VE", gve)) + P.W(f"f_p{'AC'[th]}")
                            P.emit("tensor", "matmul", pd[th][:, 0:512],
                                   s_dT[:, s, k, :], s_gatedT[:, k, tsl],
                                   start=(k == 0), stop=(k == 31), wait=w,
                                   inc=[("PE", 1)] if k == 31 else ())
                        P.emit("vector", "tensor_tensor", env.s_xres[:, dt, tsl],
                               env.s_xres[:, dt, tsl], pd[th][:, 0:512], op=ALU.add,
                               wait=P.after("PE"), inc=[("VE", 1)])
                        P.R(f"f_p{'AC'[th]}", "VE")
                    P.guards.setdefault(f"dT{s}", []).append(("PE", P.val("PE")))
                P.R("gatedT", "PE")

    P.muted = False
    x1r = x1T.rearrange("(c p) t -> p c t", p=128)
    P.emit("sync", "dma_start", out=x1r, in_=env.s_xres[:, :, :],
           wait=P.after("VE", "AC", "PE"), inc=[("ST", 16)])
    P.emit("sync", "nop", wait=P.after("ST"))
    return env



def plan_block1(nc, ctx):
    xAT = nc.dram_tensor("xAT", [DM, TPC], F32, kind="ExternalInput").ap()
    xBT = nc.dram_tensor("xBT", [DM, 640], F32, kind="ExternalInput").ap()
    Wq = nc.dram_tensor("Wq", [DM, DM], F32, kind="ExternalInput").ap()
    Wk = nc.dram_tensor("Wk", [DM, 256], F32, kind="ExternalInput").ap()
    Wv = nc.dram_tensor("Wv", [DM, 256], F32, kind="ExternalInput").ap()
    Wo = nc.dram_tensor("Wo", [DM, DM], F32, kind="ExternalInput").ap()
    bqp = nc.dram_tensor("bq_pp", [128, 16], F32, kind="ExternalInput").ap()
    bkp = nc.dram_tensor("bk_pp", [128, 2], F32, kind="ExternalInput").ap()
    bvp = nc.dram_tensor("bv_pp", [128, 2], F32, kind="ExternalInput").ap()
    bop = nc.dram_tensor("bo_pp", [128, 16], F32, kind="ExternalInput").ap()
    lut = nc.dram_tensor("lut_t", [128, 16], F32, kind="ExternalInput").ap()
    gidx = nc.dram_tensor("gate_idx", [5504, DM], I32, kind="ExternalInput").ap()
    uidx = nc.dram_tensor("up_idx", [5504, DM], I32, kind="ExternalInput").ap()
    didx = nc.dram_tensor("down_idx", [DM, 5504], I32, kind="ExternalInput").ap()
    iden = nc.dram_tensor("ident", [128, 128], F32, kind="ExternalInput").ap()
    oT = nc.dram_tensor("outT", [DM, TPC], F32, kind="ExternalOutput").ap()

    env = Env()
    env.nc = nc
    env.ctx = ctx
    env.P = P = Plan()

    env.s_xres = alloc(env, env.ctx, "xres", [128, NCH, TPC], F32)
    env.s_xnT = alloc(env, env.ctx, "xnTA", [128, NCH, TPC], BF16)
    env.ones_b = alloc(env, env.ctx, "ones_b", [128, 1], BF16)
    env.ones_f = alloc(env, env.ctx, "ones_f", [1, 128], F32)
    s_bq = alloc(env, env.ctx, "bq", [128, 16], F32)
    s_bk = alloc(env, env.ctx, "bk", [128, 2], F32)
    s_bv = alloc(env, env.ctx, "bv", [128, 2], F32)
    s_bo = alloc(env, env.ctx, "bo", [128, 16], F32)
    s_lut = alloc(env, env.ctx, "lut", [128, 16], F32)
    s_id = alloc(env, env.ctx, "id", [128, 128], BF16)

    xar = xAT.rearrange("(c p) t -> p c t", p=128)
    P.emit("scalar", "dma_start", out=env.s_xres[:, :, :], in_=xar, inc=[("LD", 16)])
    for dst, src in ((s_bq, bqp), (s_bk, bkp), (s_bv, bvp), (s_bo, bop), (s_lut, lut)):
        P.emit("scalar", "dma_start", out=dst[:, :], in_=src, inc=[("LD", 16)])
    P.emit("gpsimd", "dma_start", out=s_id[:, :], in_=iden, inc=[("LC", 16)])
    P.emit("vector", "memset", env.ones_b[:, :], 1.0, inc=[("VE", 1)])
    P.emit("vector", "memset", env.ones_f[:, :], 1.0, inc=[("VE", 1)])

    with ExitStack() as attn:
        s_xnB = alloc(env, attn, "xnTB", [128, NCH, 640], BF16)

        with ExitStack() as sub:
            emit_ln_setup(env, sub)
            emit_ln(env, "lA", env.s_xres, env.s_xnT, TPC,
                    dep=(("LD", 16),) + P.after("VE"))
            s_xB = alloc(env, sub, "xBres", [128, NCH, 640], F32)
            xbr = xBT.rearrange("(c p) t -> p c t", p=128)
            P.emit("scalar", "dma_start", out=s_xB[:, :, :], in_=xbr, inc=[("LD", 16)])
            emit_ln(env, "lB", s_xB, s_xnB, 640, dep=P.after("LD", "VE"))

        s_qT = alloc(env, attn, "qT1", [128, NCH, 640], BF16)
        s_kT = alloc(env, attn, "kT1", [128, 2, 640], BF16)
        s_vtok = alloc(env, attn, "vtok1", [128, 2, 5, 128], BF16)
        s_oT = alloc(env, attn, "oT1", [128, NCH, TPC], BF16)
        seed(P, ["vtok0", "vtok1"])
        p_A = psum(env, attn, "a_pA")
        p_B = psum(env, attn, "a_pB")
        p_C = psum(env, attn, "a_pC")
        p_D = psum(env, attn, "a_pD")
        p_E0 = psum(env, attn, "a_pE0", BF16)
        p_E1 = psum(env, attn, "a_pE1", BF16)

        with ExitStack() as sub:
            s_wb = alloc(env, sub, "wbuf1", [128, 2, NCH, 128], BF16)
            s_vT = alloc(env, sub, "vT1", [128, 2, 640], BF16)
            seed(P, ["w10", "w11"])
            wqr = Wq.rearrange("(c p) o -> p c o", p=128)
            wkr = Wk.rearrange("(c p) o -> p c o", p=128)
            wvr = Wv.rearrange("(c p) o -> p c o", p=128)
            pq = [p_A, p_C]
            NT = ((0, 512), (512, 128))
            for ob in range(20):
                s = ob & 1
                if ob < 16:
                    src = wqr[:, :, 128 * ob:128 * (ob + 1)]
                elif ob < 18:
                    g = ob - 16
                    src = wkr[:, :, 128 * g:128 * (g + 1)]
                else:
                    g = ob - 18
                    src = wvr[:, :, 128 * g:128 * (g + 1)]
                P.emit("gpsimd", "dma_start", out=s_wb[:, s, :, :], in_=src,
                       wait=P.W(f"w1{s}"), inc=[("LC", 16)])
                lc = P.val("LC")
                for nt, (t0, tw) in enumerate(NT):
                    for c in range(NCH):
                        w = ()
                        if c == 0:
                            w = (("LC", lc),) + P.W(f"a_p{'AC'[s]}")
                            if ob == 0 and nt == 0:
                                w = w + P.after("VE") + (("LD", P.val("LD")),)
                        P.emit("tensor", "matmul", pq[s][:, 0:tw],
                               s_wb[:, s, c, :], s_xnB[:, c, t0:t0 + tw],
                               start=(c == 0), stop=(c == 15), wait=w,
                               inc=[("PE", 1)] if c == 15 else ())
                    if ob < 16:
                        dst, bias = s_qT[:, ob, t0:t0 + tw], s_bq[:, ob:ob + 1]
                    elif ob < 18:
                        dst, bias = s_kT[:, ob - 16, t0:t0 + tw], s_bk[:, ob - 16:ob - 15]
                    else:
                        dst, bias = s_vT[:, ob - 18, t0:t0 + tw], s_bv[:, ob - 18:ob - 17]
                    P.emit("vector", "tensor_scalar", dst, pq[s][:, 0:tw],
                           bias, None, op0=ALU.add,
                           wait=P.after("PE"), inc=[("VE", 1)])
                    P.R(f"a_p{'AC'[s]}", "VE")
                P.guards.setdefault(f"w1{s}", []).append(("PE", P.val("PE")))
            ve = P.val("VE")
            for g in range(2):
                P.emit("sync", "dma_start_transpose", s_vtok[:, g, :, :],
                       s_vT[:, g, :], wait=(("VE", ve),) + P.W(f"vtok{g}"),
                       inc=[("TR", 16)])

        with ExitStack() as sub:
            s_nm = alloc(env, sub, "nm", [128, 2], F32)
            seed(P, ["nm0", "nm1", "au0", "au1", "an0", "an1", "at0", "at1"])
            s_bias = alloc(env, sub, "sb", [128, 2], F32)
            s_au = alloc(env, sub, "au", [128, 2, 256], BF16)
            s_rs = alloc(env, sub, "rs", [128, 2], F32)
            s_rn = alloc(env, sub, "rn", [128, 2], F32)
            s_an = alloc(env, sub, "an", [128, 2, 256], BF16)
            s_at = alloc(env, sub, "at", [128, 2, 256], BF16)
            p_sc = [p_A, p_C]
            p_at = [p_E0, p_E1]
            p_av = [p_B, p_D]
            trv = P.val("TR")
            it = 0
            for wdw in range(4):
                for h in range(16):
                    for qt in range(2):
                        s = it & 1
                        g = h & 1
                        w = P.W(f"a_p{'AC'[s]}")
                        if it == 0:
                            w = w + P.after("VE") + (("TR", trv),)
                        P.emit("tensor", "matmul", p_sc[s][:, 0:256],
                               s_qT[:, h, 128 * (wdw + qt):128 * (wdw + qt) + 128],
                               s_kT[:, g, 128 * wdw:128 * wdw + 256],
                               start=True, stop=True, wait=w, inc=[("PE", 1)])
                        P.emit("vector", "tensor_reduce", s_nm[:, s:s + 1],
                               p_sc[s][:, 0:256], axis=AX.X, op=ALU.max, negate=True,
                               wait=P.after("PE") + P.W(f"nm{s}"))
                        P.emit("vector", "tensor_scalar", s_bias[:, s:s + 1],
                               s_nm[:, s:s + 1], SCALE1, None, op0=ALU.mult,
                               inc=[("VE", 1)])
                        P.emit("scalar", "activation", s_au[:, s, :],
                               p_sc[s][:, 0:256], AF.Exp, scale=SCALE1,
                               bias=s_bias[:, s:s + 1], accum_out=s_rs[:, s:s + 1],
                               wait=P.after("VE") + P.W(f"au{s}"), inc=[("AC", 1)])
                        P.R(f"a_p{'AC'[s]}", "AC")
                        P.guards.setdefault(f"nm{s}", []).append(("AC", P.val("AC")))
                        P.emit("vector", "reciprocal", s_rn[:, s:s + 1],
                               s_rs[:, s:s + 1], wait=P.after("AC"))
                        P.emit("vector", "tensor_scalar", s_an[:, s, :],
                               s_au[:, s, :], s_rn[:, s:s + 1], None, op0=ALU.mult,
                               wait=P.W(f"an{s}"), inc=[("VE", 1)])
                        P.guards.setdefault(f"au{s}", []).append(("VE", P.val("VE")))
                        w = P.after("VE") + P.W(f"a_pE{s}")
                        P.emit("tensor", "transpose", p_at[s][:, 0:128],
                               s_an[:, s, 0:128], s_id[:, :], wait=w)
                        P.emit("tensor", "transpose", p_at[s][:, 128:256],
                               s_an[:, s, 128:256], s_id[:, :], inc=[("PE", 1)])
                        P.guards.setdefault(f"an{s}", []).append(("PE", P.val("PE")))
                        P.emit("vector", "tensor_copy", s_at[:, s, :],
                               p_at[s][:, 0:256],
                               wait=P.after("PE") + P.W(f"at{s}"), inc=[("VE", 1)])
                        P.R(f"a_pE{s}", "VE")
                        w = P.after("VE") + P.W(f"a_p{'BD'[s]}")
                        for kc in range(2):
                            P.emit("tensor", "matmul", p_av[s][:, 0:128],
                                   s_vtok[:, g, wdw + kc, :],
                                   s_at[:, s, 128 * kc:128 * kc + 128],
                                   start=(kc == 0), stop=(kc == 1),
                                   wait=w if kc == 0 else (),
                                   inc=[("PE", 1)] if kc == 1 else ())
                        P.guards.setdefault(f"at{s}", []).append(("PE", P.val("PE")))
                        P.guards.setdefault(f"vtok{g}", []).append(("PE", P.val("PE")))
                        P.emit("vector", "tensor_copy",
                               s_oT[:, h, 256 * wdw + 128 * qt:256 * wdw + 128 * qt + 128],
                               p_av[s][:, 0:128], wait=P.after("PE"), inc=[("VE", 1)])
                        P.R(f"a_p{'BD'[s]}", "VE")
                        it += 1

            s_wo = alloc(env, sub, "wobuf1", [128, 2, NCH, 128], BF16)
            seed(P, ["wo0", "wo1"])
            wor = Wo.rearrange("(c p) o -> p c o", p=128)
            pw = [p_A, p_C]
            ove = P.val("VE")
            for dt in range(NCH):
                s = dt & 1
                P.emit("gpsimd", "dma_start", out=s_wo[:, s, :, :],
                       in_=wor[:, :, 128 * dt:128 * (dt + 1)],
                       wait=P.W(f"wo{s}"), inc=[("LC", 16)])
                lc = P.val("LC")
                for th in range(2):
                    tsl = slice(512 * th, 512 * th + 512)
                    for c in range(NCH):
                        w = ()
                        if c == 0:
                            w = (("LC", lc), ("VE", ove)) + P.W(f"a_p{'AC'[s]}")
                        P.emit("tensor", "matmul", pw[s][:, 0:512],
                               s_wo[:, s, c, :], s_oT[:, c, tsl],
                               start=(c == 0), stop=(c == 15), wait=w,
                               inc=[("PE", 1)] if c == 15 else ())
                    P.emit("vector", "scalar_tensor_tensor", env.s_xres[:, dt, tsl],
                           pw[s][:, 0:512], s_bo[:, dt:dt + 1], env.s_xres[:, dt, tsl],
                           op0=ALU.add, op1=ALU.add,
                           wait=P.after("PE"), inc=[("VE", 1)])
                    P.R(f"a_p{'AC'[s]}", "VE")
                P.guards.setdefault(f"wo{s}", []).append(("PE", P.val("PE")))

    with ExitStack() as ffn:
        s_gatedT = alloc(env, ffn, "gatedT1", [128, 22, TPC], BF16)
        p_A = psum(env, ffn, "f_pA")
        p_B = psum(env, ffn, "f_pB")
        p_C = psum(env, ffn, "f_pC")
        p_D = psum(env, ffn, "f_pD")
        OH_N = (22, 21)
        OH_O = (0, 2816)
        for OH in range(2):
            nblk = OH_N[OH]
            obase = OH_O[OH]
            with ExitStack() as sub:
                s_raw = alloc(env, sub, "giraw", [128, DM], BF16)
                s_iT = alloc(env, sub, "giT", [128, NCH, 128], BF16)
                s_wbt = alloc(env, sub, "wbuilt", [128, 2, 2, NCH, 128], BF16)
                s_term = alloc(env, sub, "term", [128, DM], BF16)
                s_sg = alloc(env, sub, "sg1", [128, 2, 512], F32)
                seed(P, ["giraw", "giT", "wbuilt00", "wbuilt01", "wbuilt10",
                         "wbuilt11", "term", "sg0", "sg1"])
                pg = [p_A, p_C]
                pu = [p_B, p_D]
                for ob in range(nblk):
                    rows = slice(obase + 128 * ob, obase + 128 * (ob + 1))
                    for mi, idxs in enumerate((gidx, uidx)):
                        s = ob & 1
                        P.emit("gpsimd", "dma_start", out=s_raw[:, :],
                               in_=idxs[rows, :], wait=P.W("giraw"), inc=[("LC", 16)])
                        lc = P.val("LC")
                        P.emit("sync", "dma_start_transpose", s_iT[:, :, :],
                               s_raw[:, :], wait=(("LC", lc),) + P.W("giT"),
                               inc=[("TR", 16)])
                        P.R("giraw", "TR")
                        tr = P.val("TR")
                        wflat = s_wbt[:, s, mi, :, :].rearrange("p c o -> p (c o)")
                        P.emit("vector", "memset", wflat, 0.0,
                               wait=(("TR", tr),) + P.W(f"wbuilt{s}{mi}") + P.W("term"))
                        iflat = s_iT[:, :, :].rearrange("p c o -> p (c o)")
                        for j in range(16):
                            P.emit("vector", "tensor_scalar", s_term[:, :], iflat,
                                   float(j), s_lut[:, j:j + 1],
                                   op0=ALU.is_equal, op1=ALU.mult,
                                   wait=(("LD", P.val("LD")),) if (ob == 0 and mi == 0 and j == 0 and OH == 0) else ())
                            P.emit("vector", "tensor_tensor", wflat, wflat,
                                   s_term[:, :], op=ALU.add,
                                   inc=[("VE", 1)] if j == 15 else ())
                        P.R("giT", "VE")
                        vb = P.val("VE")
                        pp = pg if mi == 0 else pu
                        for th in range(2):
                            tsl = slice(512 * th, 512 * th + 512)
                            for c in range(NCH):
                                w = ()
                                if c == 0:
                                    w = (("VE", vb),) + P.W(f"f_p{'ACBD'[2 * mi + th]}")
                                P.emit("tensor", "matmul", pp[th][:, 0:512],
                                       s_wbt[:, s, mi, c, :], env.s_xnT[:, c, tsl],
                                       start=(c == 0), stop=(c == 15), wait=w,
                                       inc=[("PE", 1)] if c == 15 else ())
                        P.guards.setdefault(f"wbuilt{s}{mi}", []).append(("PE", P.val("PE")))
                        P.guards.setdefault("term", []).append(("VE", vb))
                    for th in range(2):
                        tsl = slice(512 * th, 512 * th + 512)
                        P.emit("scalar", "activation", s_sg[:, th, :],
                               pg[th][:, 0:512], AF.Silu,
                               wait=P.after("PE") + P.W(f"sg{th}"), inc=[("AC", 1)])
                        P.R(f"f_p{'AC'[th]}", "AC")
                        P.emit("vector", "tensor_tensor", s_gatedT[:, ob, tsl],
                               s_sg[:, th, :], pu[th][:, 0:512], op=ALU.mult,
                               wait=P.after("AC"), inc=[("VE", 1)])
                        P.R(f"f_p{'BD'[th]}", "VE")
                        P.guards.setdefault(f"sg{th}", []).append(("VE", P.val("VE")))
            with ExitStack() as sub:
                s_draw = alloc(env, sub, "diraw", [128, nblk * 128], BF16)
                s_diT = alloc(env, sub, "diT", [128, nblk, 128], BF16)
                s_wdb = alloc(env, sub, "wdb", [128, 2, 22, 128], BF16)
                s_term = alloc(env, sub, "term2", [128, nblk * 128], BF16)
                seed(P, ["diraw", "diT", "wdb0", "wdb1", "term2"])
                pd = [p_A, p_C]
                gve = P.val("VE")
                for dt in range(NCH):
                    s = dt & 1
                    P.emit("gpsimd", "dma_start", out=s_draw[:, :],
                           in_=didx[128 * dt:128 * (dt + 1), obase:obase + nblk * 128],
                           wait=P.W("diraw"), inc=[("LC", 16)])
                    lc = P.val("LC")
                    P.emit("sync", "dma_start_transpose", s_diT[:, :, :],
                           s_draw[:, :], wait=(("LC", lc),) + P.W("diT"),
                           inc=[("TR", 16)])
                    P.R("diraw", "TR")
                    tr = P.val("TR")
                    wflat = s_wdb[:, s, 0:nblk, :].rearrange("p c o -> p (c o)")
                    P.emit("vector", "memset", wflat, 0.0,
                           wait=(("TR", tr),) + P.W(f"wdb{s}") + P.W("term2"))
                    iflat = s_diT[:, :, :].rearrange("p c o -> p (c o)")
                    for j in range(16):
                        P.emit("vector", "tensor_scalar", s_term[:, :], iflat,
                               float(j), s_lut[:, j:j + 1],
                               op0=ALU.is_equal, op1=ALU.mult)
                        P.emit("vector", "tensor_tensor", wflat, wflat,
                               s_term[:, :], op=ALU.add,
                               inc=[("VE", 1)] if j == 15 else ())
                    P.R("diT", "VE")
                    vb = P.val("VE")
                    for th in range(2):
                        tsl = slice(512 * th, 512 * th + 512)
                        for k in range(nblk):
                            w = ()
                            if k == 0:
                                w = (("VE", max(vb, gve)),) + P.W(f"f_p{'AC'[th]}")
                            P.emit("tensor", "matmul", pd[th][:, 0:512],
                                   s_wdb[:, s, k, :], s_gatedT[:, k, tsl],
                                   start=(k == 0), stop=(k == nblk - 1), wait=w,
                                   inc=[("PE", 1)] if k == nblk - 1 else ())
                        P.emit("vector", "tensor_tensor", env.s_xres[:, dt, tsl],
                               env.s_xres[:, dt, tsl], pd[th][:, 0:512], op=ALU.add,
                               wait=P.after("PE"), inc=[("VE", 1)])
                        P.R(f"f_p{'AC'[th]}", "VE")
                    P.guards.setdefault(f"wdb{s}", []).append(("PE", P.val("PE")))
                    P.guards.setdefault("term2", []).append(("VE", vb))

    orr = oT.rearrange("(c p) t -> p c t", p=128)
    P.emit("sync", "dma_start", out=orr, in_=env.s_xres[:, :, :],
           wait=P.after("VE", "AC", "PE"), inc=[("ST", 16)])
    P.emit("sync", "nop", wait=P.after("ST"))
    return env



def _build_mask0():
    m = np.full((128, 144), NEG, np.float32)
    for q in range(128):
        sseg, j = q // 32, q % 32
        for c in range(128):
            if c // 32 == sseg and (c % 32) <= j - 16:
                m[q, c] = 0.0
        for p in range(16):
            if p <= j:
                m[q, 128 + p] = 0.0
    return m


def _pp(v, n):
    return np.ascontiguousarray(np.asarray(v, np.float32).reshape(n, 128).T)


def kernel(**inp):
    x = np.asarray(inp["L0_attn_seq"], np.float32)
    pm = np.asarray(inp["L0_attn_pm"], np.float32)
    pmkT = np.ascontiguousarray(
        pm[0].reshape(16, 2, 16, 64).transpose(1, 3, 0, 2).reshape(128, 256))
    pmvT = np.ascontiguousarray(pm[1].transpose(1, 0, 2).reshape(16, 2048))
    wn_pp = _pp(inp["L0_attn_w_norm"], 16)
    mask0 = _build_mask0()
    ident = np.eye(128, dtype=np.float32)

    with ExitStack() as ctx:
        nc0 = bass.Bass("TRN2", target_bir_lowering=False)
        plan0 = plan_block0(nc0, ctx)
        playback(nc0, ctx, plan0.P)
        in0 = []
        for core in range(NCORE):
            b, m = divmod(core, 4)
            x0T = np.ascontiguousarray(x[b, 1024 * m:1024 * (m + 1), :].T)
            in0.append({
                "xT": x0T,
                "w_qkv": inp["L0_attn_w_qkv"],
                "w_out": inp["L0_attn_w_out"],
                "w_norm_pp": wn_pp,
                "pmkT": pmkT,
                "pmvT": pmvT,
                "w_gate": inp["L0_ffn_w_gate"],
                "w_up": inp["L0_ffn_w_up"],
                "w_down": inp["L0_ffn_w_down"],
                "mask0": mask0,
                "ident": ident,
            })
        res0 = run_bass_kernel_spmd(nc0, in0, core_ids=list(range(NCORE)))
    x1f = [np.concatenate([res0.results[4 * b + m]["x1T"] for m in range(4)], axis=1)
           for b in range(2)]

    with ExitStack() as ctx:
        nc1 = bass.Bass("TRN2", target_bir_lowering=False)
        plan1 = plan_block1(nc1, ctx)
        playback(nc1, ctx, plan1.P)
        in1 = []
        for core in range(NCORE):
            b, m = divmod(core, 4)
            in1.append({
                "xAT": np.ascontiguousarray(x1f[b][:, 1024 * m:1024 * (m + 1)]),
                "xBT": np.ascontiguousarray(x1f[b][:, 512 * m:512 * m + 640]),
                "Wq": inp["L1_attn_Wq"],
                "Wk": inp["L1_attn_Wk"],
                "Wv": inp["L1_attn_Wv"],
                "Wo": inp["L1_attn_Wo"],
                "bq_pp": _pp(inp["L1_attn_bq"], 16),
                "bk_pp": _pp(inp["L1_attn_bk"], 2),
                "bv_pp": _pp(inp["L1_attn_bv"], 2),
                "bo_pp": _pp(inp["L1_attn_bo"], 16),
                "lut_t": np.ascontiguousarray(
                    np.tile(np.asarray(inp["L1_ffn_lut"], np.float32)[None, :], (128, 1))),
                "gate_idx": inp["L1_ffn_gate_idx"],
                "up_idx": inp["L1_ffn_up_idx"],
                "down_idx": inp["L1_ffn_down_idx"],
                "ident": ident,
            })
        res1 = run_bass_kernel_spmd(nc1, in1, core_ids=list(range(NCORE)))

    outf = [np.concatenate([res1.results[4 * b + m]["outT"] for m in range(4)], axis=1)
            for b in range(2)]
    out = np.stack([outf[0].T, outf[1].T], axis=0)
    return np.ascontiguousarray(out).astype(np.float32)
